# version 07
# speedup vs baseline: 1.5973x; 1.0148x over previous
import numpy as np
import ml_dtypes
from contextlib import ExitStack

import concourse.bass as bass
import concourse.mybir as mybir
from concourse.bass_utils import run_bass_kernel_spmd
from concourse.bass_core_rust import IndirectOffsetOnAxis

F32 = mybir.dt.float32
BF16 = mybir.dt.bfloat16
U32 = mybir.dt.uint32
AF = mybir.ActivationFunctionType
ALU = mybir.AluOpType
AX = mybir.AxisListType

N_TOK = 16384
D = 1024
H = 4096
E = 8
CAP = N_TOK // E
NT = N_TOK // 128
NCORES = 8
EPS = 1e-5
C2G = float(2.0 * 0.7978845608028654)
SQF = float(np.sqrt(np.float32(0.044715)))
OOB = 16383


class Ctr:
    def __init__(self, h):
        self.h, self.n = h, 0

    def inc(self, k):
        self.n += k
        return (self.h, k)


class DPool:

    def __init__(self, nc, ctx, name, k):
        self.cs = [Ctr(ctx.enter_context(nc.semaphore(f"{name}{i}"))) for i in range(k)]
        self.i = 0

    def issue(self, eng):
        c = self.cs[self.i]
        self.i = (self.i + 1) % len(self.cs)
        if c.n:
            eng.wait_ge(c.h, c.n)
        c.n += 16
        return (c.h, 16), (c.h, c.n)

    def snap(self):
        return [(c.h, c.n) for c in self.cs]


def build(n_phases=99, debug=False):
    nc = bass.Bass("TRN2", target_bir_lowering=False)

    x_d = nc.dram_tensor("x", [N_TOK, D], F32, kind="ExternalInput").ap()
    alog_d = nc.dram_tensor("alog", [64, 1], F32, kind="ExternalInput").ap()
    dmat_d = nc.dram_tensor("dmat", [64, D], F32, kind="ExternalInput").ap()
    win_d = nc.dram_tensor("win", [2 * D, D], F32, kind="ExternalInput").ap()
    bin_d = nc.dram_tensor("bin", [1, 2 * D], F32, kind="ExternalInput").ap()
    wxp_d = nc.dram_tensor("wxp", [64, D], F32, kind="ExternalInput").ap()
    bxp_d = nc.dram_tensor("bxp", [64, 1], F32, kind="ExternalInput").ap()
    wout_d = nc.dram_tensor("wout", [D, D], F32, kind="ExternalInput").ap()
    bout_d = nc.dram_tensor("bout", [1, D], F32, kind="ExternalInput").ap()
    gam_d = nc.dram_tensor("gam", [128, 8], F32, kind="ExternalInput").ap()
    bet_d = nc.dram_tensor("bet", [128, 8], F32, kind="ExternalInput").ap()
    wg_d = nc.dram_tensor("wg", [D, E], F32, kind="ExternalInput").ap()
    w1_d = nc.dram_tensor("w1", [E, D, H], F32, kind="ExternalInput").ap()
    w2_d = nc.dram_tensor("w2", [E, H, D], F32, kind="ExternalInput").ap()
    ones_d = nc.dram_tensor("ones", [128, 128], F32, kind="ExternalInput").ap()
    sut_d = nc.dram_tensor("sut", [128, 128], F32, kind="ExternalInput").ap()
    ident_d = nc.dram_tensor("ident", [128, 128], F32, kind="ExternalInput").ap()
    rank_d = nc.dram_tensor("rankin", [1, 1], U32, kind="ExternalInput").ap()
    bnd_d = nc.dram_tensor("bnd", [1, 1], U32, kind="ExternalInput").ap()
    zfill_d = nc.dram_tensor("zfill", [2048, 1024], BF16, kind="ExternalInput").ap()

    out_d = nc.dram_tensor("out", [CAP, D], F32, kind="ExternalOutput").ap()

    xbd = nc.dram_tensor("xbd", [N_TOK, D], BF16).ap()
    winb = nc.dram_tensor("winb", [2 * D, D], BF16).ap()
    woutb = nc.dram_tensor("woutb", [D, D], BF16).ap()
    wxpb = nc.dram_tensor("wxpb", [64, D], BF16).ap()
    wintp = nc.dram_tensor("wintp", [128, 8 * 2048], BF16).ap()
    boutp = nc.dram_tensor("boutp", [128, D], F32).ap()
    w1b = nc.dram_tensor("w1b", [E, D, H], BF16).ap()
    w2b = nc.dram_tensor("w2b", [E, H, D], BF16).ap()
    xg1 = nc.dram_tensor("xg1", [N_TOK, D], BF16).ap()
    y1d = nc.dram_tensor("y1d", [N_TOK, D], BF16).ap()
    xg2 = nc.dram_tensor("xg2", [N_TOK, D], BF16).ap()
    y2d = nc.dram_tensor("y2d", [N_TOK, D], BF16).ap()
    x1_dram = nc.dram_tensor("x1_dram", [N_TOK, D], F32).ap()
    x2_dram = nc.dram_tensor("x2_dram", [N_TOK, D], F32).ap()
    x3_dram = nc.dram_tensor("x3_dram", [N_TOK, D], F32).ap()
    xn2b = nc.dram_tensor("xn2b", [N_TOK, D], BF16).ap()
    x2b = nc.dram_tensor("x2b", [N_TOK, D], BF16).ap()
    row_bounce = nc.dram_tensor("row_bounce", [2, N_TOK], F32).ap()
    sml_bounce = nc.dram_tensor("sml_bounce", [2048], F32).ap()

    dbg = {}

    def dbg_out(name, shape, dtype=F32):
        dbg[name] = nc.dram_tensor(name, shape, dtype, kind="ExternalOutput").ap()
        return dbg[name]

    with ExitStack() as ctx:
        def sb(name, shape, dtype=F32, c=None):
            return (c or ctx).enter_context(nc.sbuf_tensor(name, shape, dtype))

        ONES = sb("ONES", [128, 128])
        SUT = sb("SUTs", [128, 128])
        IDENT = sb("IDENTs", [128, 128])
        WGS = sb("WGS", [128, 64])
        WGB = sb("WGB", [128, 64], BF16)
        GAM = sb("GAMs", [128, 8])
        BETF = sb("BETF", [128, 8])
        BET = sb("BETs", [128, 8], BF16)
        BINPP = sb("BINPP", [128, 16])
        AE = sb("AEs", [64, 1])
        AA = sb("AAs", [64, 1])
        BXP = sb("BXPs", [64, 1])
        DMATB = sb("DMATB", [64, 1024], BF16)
        CSC = sb("CSCs", [8, 1])
        RANKS = sb("RANKS", [1, 1], U32)
        BNDS = sb("BNDS", [1, 1], U32)
        EPSC = sb("EPSC", [128, 1])
        TOKB = sb("TOKB", [128, 128])
        SUMS = sb("SUMS", [128, NT])
        SSQ = sb("SSQ", [128, NT])
        MU = sb("MUs", [128, NT])
        RSTD = sb("RSTDs", [128, NT])
        NEGR = sb("NEGR", [128, NT])
        MU1 = sb("MU1s", [128, NT])
        RSTD1 = sb("RSTD1s", [128, NT])
        S1 = sb("S1s", [128, NT])
        S2 = sb("S2s", [128, NT])
        MU2 = sb("MU2s", [128, NT])
        RSTD2 = sb("RSTD2s", [128, NT])
        MU3 = sb("MU3s", [128, NT])
        RSTD3 = sb("RSTD3s", [128, NT])
        NEGR3 = sb("NEGR3", [128, NT])
        KG1 = sb("KG1s", [128, NT])
        KG2 = sb("KG2s", [128, NT])
        SLOT1 = sb("SLOT1s", [128, NT], U32)
        SLOT2 = sb("SLOT2s", [128, NT], U32)
        TMPC = [sb(f"TMPC{i}", [128, NT]) for i in range(4)]
        LOG = sb("LOGs", [128, 1024])
        FP = [sb(f"FP{i}", [128, 1024]) for i in range(8)]
        BP = [sb(f"BP{i}", [128, 1024], BF16) for i in range(4)]
        XBUF = [FP[0], FP[1]]
        SCRA = FP[2]
        SCRV = FP[3]
        X1BUF = [FP[4], FP[5]]
        TBUF = [FP[6], FP[7]]
        XNBUF = [BP[0], BP[1]]
        YGBUF = [BP[2], BP[3]]

        QP = DPool(nc, ctx, "q", 8)
        GP = DPool(nc, ctx, "g", 8)
        SP = DPool(nc, ctx, "w", 4)
        sV = Ctr(ctx.enter_context(nc.semaphore("sV")))
        sA = Ctr(ctx.enter_context(nc.semaphore("sA")))
        sT = Ctr(ctx.enter_context(nc.semaphore("sT")))
        sP = Ctr(ctx.enter_context(nc.semaphore("sP")))

        block = ctx.enter_context(nc.Block())

        def snap():
            return (QP.snap() + GP.snap() + SP.snap()
                    + [(c.h, c.n) for c in (sV, sA, sT, sP)])

        def wait_snap(eng, sn):
            for h, n in sn:
                if n:
                    eng.wait_ge(h, n)

        def dma(eng, pool, out, in_):
            inc, tok = pool.issue(eng)
            eng.dma_start(out=out, in_=in_).then_inc(*inc)
            return tok

        def dmat_(eng, pool, out, in_):
            inc, tok = pool.issue(eng)
            eng.dma_start_transpose(out, in_).then_inc(*inc)
            return tok

        def finalize():
            sn = snap()

            @block.sync
            def _(sync):
                wait_snap(sync, sn)
                tok = dma(sync, QP, out_d[:], x_d[0:CAP, :])
                sync.wait_ge(*tok)

        ph0_toks = []

        @block.sync
        def _(sync):
            for t, d in [(ONES, ones_d), (SUT, sut_d), (IDENT, ident_d),
                         (GAM, gam_d), (BETF, bet_d),
                         (AE, alog_d), (BXP, bxp_d), (RANKS, rank_d)]:
                ph0_toks.append(dma(sync, QP, t[:], d[:]))
            ph0_toks.append(dma(sync, QP, FP[0][0:64, :], dmat_d[:]))
            ph0_toks.append(dma(sync, QP, WGS[:].rearrange("p (c e) -> p c e", e=8),
                                wg_d[:].rearrange("(c p) e -> p c e", p=128)))
        sn_ph0 = snap()

        cast_toks = {}
        r_bnd = [None]

        @block.gpsimd
        def _(g):
            g.iota(TOKB[:], [[128, NT]], base=100000, channel_multiplier=1,
                   allow_small_or_imprecise_dtypes=True).then_inc(*sP.inc(1))
            tok = dma(g, GP, BNDS[:], bnd_d[:])
            g.wait_ge(*tok)
            r_bnd[0] = g.alloc_register("r_bnd")
            g.reg_load(r_bnd[0], BNDS[0:1, 0:1])
            cast_toks["xbd"] = dma(g, GP, xbd[:], x_d[:])
            for cc in range(8):
                dma(g, GP, xg1[cc * 2048:(cc + 1) * 2048, :], zfill_d[:])
            cast_toks["winb"] = dma(g, GP, winb[:], win_d[:])
            cast_toks["woutb"] = dma(g, GP, woutb[:], wout_d[:])
            cast_toks["wxpb"] = dma(g, GP, wxpb[:], wxp_d[:])
            cast_toks["w1b0"] = [dma(g, GP, w1b[0], w1_d[0])]
            cast_toks["w2b0"] = [dma(g, GP, w2b[0], w2_d[0])]
            for cc in range(8):
                dma(g, GP, xg2[cc * 2048:(cc + 1) * 2048, :], zfill_d[:])

        @block.vector
        def _(v):
            wait_snap(v, sn_ph0)
            v.memset(EPSC[:], float(EPS)).then_inc(*sV.inc(1))
            v.memset(XNBUF[0][:], 0.0).then_inc(*sV.inc(1))
            v.memset(XNBUF[1][:], 0.0).then_inc(*sV.inc(1))
            v.memset(YGBUF[0][:], 0.0).then_inc(*sV.inc(1))
            v.memset(YGBUF[1][:], 0.0).then_inc(*sV.inc(1))
            v.tensor_copy(DMATB[:], FP[0][0:64, :]).then_inc(*sV.inc(1))
            v.tensor_copy(WGB[:], WGS[:]).then_inc(*sV.inc(1))
            v.tensor_copy(BET[:], BETF[:]).then_inc(*sV.inc(1))
        ph0_v = sV.n

        @block.scalar
        def _(s):
            wait_snap(s, sn_ph0)
            s.activation(AE[:], AE[:], AF.Exp).then_inc(*sA.inc(1))
            s.wait_ge(sA.h, sA.n)
            s.activation(AA[:], AE[:], AF.Exp, scale=-1.0).then_inc(*sA.inc(1))

        with ExitStack() as c0:
            BIN = sb("BINs", [1, 2048], F32, c0)
            BOUT = sb("BOUTs", [1, 1024], F32, c0)
            MROW = sb("MROW0", [1, 2048], F32, c0)
            NROW = sb("NROW0", [1, 2048], F32, c0)
            CHSROW = sb("CHSROW0", [1, 1024], F32, c0)
            WINT = sb("WINT0", [128, 8 * 2048], BF16, c0)
            PBIN = c0.enter_context(nc.psum_tensor("PBIN", [1, 2048], F32))
            PCS = c0.enter_context(nc.psum_tensor("PCS", [1, 8], F32))
            PBOUT = c0.enter_context(nc.psum_tensor("PBOUT", [128, 1024], F32))

            tk0 = {}

            @block.sync
            def _(sync):
                tk0["bin"] = dma(sync, QP, BIN[:], bin_d[:])
                tk0["bout"] = dma(sync, QP, BOUT[:], bout_d[:])
                sync.wait_ge(*cast_toks["winb"])
                for dc in range(8):
                    tk0[f"wint{dc}"] = dmat_(sync, QP, WINT[:, dc * 2048:(dc + 1) * 2048],
                                             winb[:, dc * 128:(dc + 1) * 128])

            @block.tensor
            def _(t):
                for k in tk0:
                    t.wait_ge(*tk0[k])
                t.wait_ge(sV.h, ph0_v)
                for grp in range(4):
                    for dc in range(8):
                        mm = t.matmul(PBIN[:, grp * 512:(grp + 1) * 512],
                                      BET[:, dc:dc + 1],
                                      WINT[:, dc * 2048 + grp * 512: dc * 2048 + (grp + 1) * 512],
                                      start=(dc == 0), stop=(dc == 7))
                mm.then_inc(*sT.inc(1))
                for c in range(8):
                    mm = t.matmul(PCS[:], ONES[:, 0:1], WGS[:, c * 8:(c + 1) * 8],
                                  start=(c == 0), stop=(c == 7))
                mm.then_inc(*sT.inc(1))
                for grp in range(2):
                    mm = t.matmul(PBOUT[:, grp * 512:(grp + 1) * 512], ONES[0:1, :],
                                  BOUT[:, grp * 512:(grp + 1) * 512], start=True, stop=True)
                    mm.then_inc(*sT.inc(1))
            ph0_t = sT.n

            @block.vector
            def _(v):
                v.wait_ge(sT.h, ph0_t)
                v.tensor_add(MROW[0:1, 0:1024], PBIN[0:1, 0:1024], BIN[0:1, 0:1024]).then_inc(*sV.inc(1))
                v.tensor_add(NROW[0:1, 0:1024], PBIN[0:1, 1024:2048], BIN[0:1, 1024:2048]).then_inc(*sV.inc(1))
                v.tensor_copy(FP[0][:], PBOUT[:]).then_inc(*sV.inc(1))
                v.tensor_copy(CHSROW[0:1, 0:8], PCS[0:1, :]).then_inc(*sV.inc(1))
                for dc in range(8):
                    v.tensor_scalar(WINT[:, dc * 2048:(dc + 1) * 2048],
                                    WINT[:, dc * 2048:(dc + 1) * 2048],
                                    GAM[:, dc:dc + 1], None, ALU.mult).then_inc(*sV.inc(1))
            ph0e_v = sV.n

            ph0g_t0 = sT.n
            ph0g_v0 = sV.n

            @block.tensor
            def _(t):
                t.wait_ge(sV.h, ph0e_v)
                for ft in range(16):
                    if ft >= 2:
                        t.wait_ge(sV.h, ph0g_v0 + ft - 1)
                    src = MROW if ft < 8 else NROW
                    off = (ft % 8) * 128
                    dst = PBOUT[:, 0:1] if ft % 2 == 0 else PBOUT[:, 512:513]
                    t.transpose(dst, src[0:1, off:off + 128],
                                IDENT[0:1, 0:1]).then_inc(*sT.inc(1))

            @block.vector
            def _(v):
                for ft in range(16):
                    v.wait_ge(sT.h, ph0g_t0 + ft + 1)
                    src2 = PBOUT[:, 0:1] if ft % 2 == 0 else PBOUT[:, 512:513]
                    v.tensor_copy(BINPP[:, ft:ft + 1], src2).then_inc(*sV.inc(1))
            ph0g_v = sV.n

            @block.sync
            def _(sync):
                sync.wait_ge(sV.h, ph0g_v)
                t1 = dma(sync, QP, sml_bounce[0:8], CHSROW[0:1, 0:8])
                sync.wait_ge(*t1)
                t2 = dma(sync, QP, CSC[:],
                         sml_bounce[0:8].rearrange("(e one) -> e one", one=1))
                t3 = dma(sync, QP, wintp[:], WINT[:])
                t4 = dma(sync, QP, boutp[:], FP[0][:])
                for t in (t2, t3, t4):
                    sync.wait_ge(*t)

        def stats_pass(src_dram):
            sn = snap()
            v0, a0 = sV.n, sA.n
            NBLK = NT // 8
            toks = [None] * NBLK
            with ExitStack() as c:
                SRC = [sb(f"STSRC{i}", [128, 8 * 1024], F32, c) for i in range(2)]

                @block.sync
                def _(sync):
                    wait_snap(sync, sn)
                    for k in range(NBLK):
                        if k >= 2:
                            sync.wait_ge(sV.h, v0 + 8 * (k - 1))
                            sync.wait_ge(sA.h, a0 + 8 * (k - 1))
                        toks[k] = dma(sync, QP,
                                      SRC[k % 2][:].rearrange("p (t d) -> p t d", d=1024),
                                      src_dram[k * 1024:(k + 1) * 1024, :]
                                      .rearrange("(t p) d -> p t d", p=128))

                @block.scalar
                def _(s):
                    wait_snap(s, sn)
                    for i in range(NT):
                        if i % 8 == 0:
                            s.wait_ge(*toks[i // 8])
                        if i >= 1:
                            s.wait_ge(sA.h, sA.n)
                        sl = SRC[(i // 8) % 2][:, (i % 8) * 1024:(i % 8 + 1) * 1024]
                        s.activation(SCRA[:], sl, AF.Identity,
                                     accum_out=SUMS[:, i:i + 1]).then_inc(*sA.inc(1))

                @block.vector
                def _(v):
                    wait_snap(v, sn)
                    for i in range(NT):
                        if i % 8 == 0:
                            v.wait_ge(*toks[i // 8])
                        if i >= 1:
                            v.wait_ge(sV.h, sV.n)
                        sl = SRC[(i // 8) % 2][:, (i % 8) * 1024:(i % 8 + 1) * 1024]
                        v.scalar_tensor_tensor(SCRV[:], sl, 1.0, sl,
                                               ALU.mult, ALU.mult,
                                               accum_out=SSQ[:, i:i + 1]).then_inc(*sV.inc(1))
                sn2x = snap()

                @block.sync
                def _(sync):
                    wait_snap(sync, sn2x)

        def stats_finalize(MU_, RSTD_, NEGR_=None, S1_=None, S2_=None,
                           R2_=None, F1_=None):
            a_bar = sA.n
            v0, a0 = sV.n, sA.n

            @block.vector
            def _(v):
                v.wait_ge(sA.h, a_bar)
                v.wait_ge(sV.h, sV.n)
                v.tensor_scalar(MU_[:], SUMS[:], 1.0 / D, None, ALU.mult).then_inc(*sV.inc(1))
                v.tensor_scalar(TMPC[0][:], SSQ[:], 1.0 / D, None, ALU.mult).then_inc(*sV.inc(1))
                v.wait_ge(sV.h, sV.n)
                v.tensor_mul(TMPC[1][:], MU_[:], MU_[:]).then_inc(*sV.inc(1))
                v.wait_ge(sV.h, sV.n)
                v.tensor_sub(TMPC[2][:], TMPC[0][:], TMPC[1][:]).then_inc(*sV.inc(1))

            @block.scalar
            def _(s):
                s.wait_ge(sV.h, v0 + 4)
                s.activation(TMPC[3][:], TMPC[2][:], AF.Ln, bias=EPSC[:]).then_inc(*sA.inc(1))
                s.wait_ge(sA.h, sA.n)
                s.activation(RSTD_[:], TMPC[3][:], AF.Exp, scale=-0.5).then_inc(*sA.inc(1))

            @block.vector
            def _(v):
                v.wait_ge(sA.h, a0 + 2)
                if NEGR_ is not None:
                    v.tensor_scalar(NEGR_[:], RSTD_[:], -1.0, None, ALU.mult).then_inc(*sV.inc(1))
                if S1_ is not None:
                    v.tensor_scalar(S1_[:], RSTD_[:], 1.0, None, ALU.add).then_inc(*sV.inc(1))
                    v.tensor_mul(S2_[:], MU_[:], RSTD_[:]).then_inc(*sV.inc(1))
                if R2_ is not None:
                    v.tensor_mul(TMPC[1][:], RSTD_[:], RSTD_[:]).then_inc(*sV.inc(1))
                    v.wait_ge(sV.h, sV.n)
                    v.tensor_mul(TMPC[1][:], TMPC[2][:], TMPC[1][:]).then_inc(*sV.inc(1))
            v1 = sV.n
            a1 = sA.n

            if R2_ is not None:
                @block.scalar
                def _(s):
                    s.wait_ge(sV.h, v1)
                    s.activation(TMPC[3][:], TMPC[1][:], AF.Ln, bias=EPSC[:]).then_inc(*sA.inc(1))
                    s.wait_ge(sA.h, sA.n)
                    s.activation(R2_[:], TMPC[3][:], AF.Exp, scale=-0.5).then_inc(*sA.inc(1))

                @block.vector
                def _(v):
                    v.wait_ge(sA.h, a1 + 2)
                    v.tensor_mul(F1_[:], RSTD_[:], R2_[:]).then_inc(*sV.inc(1))

        def logits_pass(src_bf_dram, MU_, NEGR_, tag):
            sn = snap()
            with ExitStack() as c:
                XTS = [sb(f"LXT{i}{tag}", [128, 2048], BF16, c) for i in range(8)]
                LGS = sb(f"LGS{tag}", [8, 2048], F32, c)
                MUBS = sb(f"MUBS{tag}", [8, 2048], F32, c)
                RBS = sb(f"RBS{tag}", [8, 2048], F32, c)
                T1S = sb(f"T1S{tag}", [8, 2048], F32, c)
                MROW = sb(f"MROWl{tag}", [1, 2048], F32, c)
                NROW = sb(f"NROWl{tag}", [1, 2048], F32, c)
                PLG = c.enter_context(nc.psum_tensor(f"PLG{tag}", [8, 2048], F32))
                PTLa = c.enter_context(nc.psum_tensor(f"PTLa{tag}", [128, 8], F32))
                PTLb = c.enter_context(nc.psum_tensor(f"PTLb{tag}", [128, 8], F32))
                PMT = c.enter_context(nc.psum_tensor(f"PMT{tag}", [128, 128], F32))

                tr_t0 = sT.n
                tr_v0 = sV.n

                @block.tensor
                def _(t):
                    wait_snap(t, sn)
                    t.transpose(PMT[:], MU_[:], IDENT[:]).then_inc(*sT.inc(1))
                    t.wait_ge(sV.h, tr_v0 + 1)
                    t.transpose(PMT[:], NEGR_[:], IDENT[:]).then_inc(*sT.inc(1))

                @block.vector
                def _(v):
                    wait_snap(v, sn)
                    v.wait_ge(sT.h, tr_t0 + 1)
                    v.tensor_copy(TMPC[0][:], PMT[:]).then_inc(*sV.inc(1))
                    v.wait_ge(sT.h, tr_t0 + 2)
                    v.tensor_copy(TMPC[1][:], PMT[:]).then_inc(*sV.inc(1))
                tr_v = sV.n

                rows_toks = []

                @block.sync
                def _(sync):
                    wait_snap(sync, sn)
                    sync.wait_ge(sV.h, tr_v)
                    rows_toks.append(dma(sync, QP,
                                         row_bounce[0].rearrange("(c p) -> c p", p=128),
                                         TMPC[0][:]))
                    rows_toks.append(dma(sync, QP,
                                         row_bounce[1].rearrange("(c p) -> c p", p=128),
                                         TMPC[1][:]))

                t_prev = [sT.n]
                for strip in range(8):
                    ld_toks = []
                    v_prev_end = sV.n

                    @block.sync
                    def _(sync, strip=strip, ld_toks=ld_toks):
                        if strip == 0:
                            for tkx in rows_toks:
                                sync.wait_ge(*tkx)
                        else:
                            sync.wait_ge(sT.h, t_prev[0])
                        for dc in range(8):
                            ld_toks.append(dmat_(sync, QP, XTS[dc][:],
                                                 src_bf_dram[strip * 2048:(strip + 1) * 2048,
                                                             dc * 128:(dc + 1) * 128]))
                        ld_toks.append(dma(sync, QP, MROW[0:1, :],
                                           row_bounce[0, strip * 2048:(strip + 1) * 2048]
                                           .unsqueeze(0)))
                        ld_toks.append(dma(sync, QP, NROW[0:1, :],
                                           row_bounce[1, strip * 2048:(strip + 1) * 2048]
                                           .unsqueeze(0)))

                    @block.tensor
                    def _(t, strip=strip, ld_toks=ld_toks, v_prev_end=v_prev_end):
                        for tkx in ld_toks:
                            t.wait_ge(*tkx)
                        if strip > 0:
                            t.wait_ge(sV.h, v_prev_end)
                        for sub in range(4):
                            for dc in range(8):
                                mm = t.matmul(PLG[:, sub * 512:(sub + 1) * 512],
                                              WGB[:, dc * 8:(dc + 1) * 8],
                                              XTS[dc][:, sub * 512:(sub + 1) * 512],
                                              start=(dc == 0), stop=(dc == 7))
                        mm.then_inc(*sT.inc(1))
                    t_mm = sT.n

                    @block.vector
                    def _(v, t_mm=t_mm):
                        v.wait_ge(sT.h, t_mm)
                        v.tensor_copy(LGS[:], PLG[:]).then_inc(*sV.inc(1))
                    ev1n = sV.n

                    @block.tensor
                    def _(t, ev1n=ev1n):
                        t.wait_ge(sV.h, ev1n)
                        for sub in range(4):
                            mm = t.matmul(PLG[:, sub * 512:(sub + 1) * 512], ONES[0:1, 0:8],
                                          MROW[0:1, sub * 512:(sub + 1) * 512],
                                          start=True, stop=True)
                        mm.then_inc(*sT.inc(1))
                    t_mb = sT.n

                    @block.vector
                    def _(v, t_mb=t_mb):
                        v.wait_ge(sT.h, t_mb)
                        v.tensor_copy(MUBS[:], PLG[:]).then_inc(*sV.inc(1))
                    ev2n = sV.n

                    @block.tensor
                    def _(t, ev2n=ev2n):
                        t.wait_ge(sV.h, ev2n)
                        for sub in range(4):
                            mm = t.matmul(PLG[:, sub * 512:(sub + 1) * 512], ONES[0:1, 0:8],
                                          NROW[0:1, sub * 512:(sub + 1) * 512],
                                          start=True, stop=True)
                        mm.then_inc(*sT.inc(1))
                    t_nb = sT.n
                    t_prev[0] = t_nb

                    @block.vector
                    def _(v, t_nb=t_nb):
                        v.wait_ge(sT.h, t_nb)
                        v.tensor_copy(RBS[:], PLG[:]).then_inc(*sV.inc(1))
                        v.wait_ge(sV.h, sV.n)
                        v.scalar_tensor_tensor(T1S[:], MUBS[:], CSC[:], LGS[:],
                                               ALU.mult, ALU.subtract).then_inc(*sV.inc(1))
                        v.wait_ge(sV.h, sV.n)
                        v.tensor_mul(LGS[:], T1S[:], RBS[:]).then_inc(*sV.inc(1))
                    lg_v = sV.n

                    t_tr0 = sT.n

                    @block.tensor
                    def _(t, lg_v=lg_v):
                        t.wait_ge(sV.h, lg_v)
                        for tt in range(16):
                            if tt >= 2:
                                t.wait_ge(sV.h, lg_v + tt - 1)
                            dst = PTLa if tt % 2 == 0 else PTLb
                            t.transpose(dst[:],
                                        LGS[0:8, tt * 128:(tt + 1) * 128],
                                        IDENT[0:8, 0:8]).then_inc(*sT.inc(1))

                    @block.vector
                    def _(v, strip=strip, t_tr0=t_tr0):
                        for tt in range(16):
                            v.wait_ge(sT.h, t_tr0 + tt + 1)
                            src = PTLa if tt % 2 == 0 else PTLb
                            cc = strip * 16 + tt
                            v.tensor_copy(LOG[:, cc * 8:(cc + 1) * 8], src[:]).then_inc(*sV.inc(1))
                sn2 = snap()

                @block.sync
                def _(sync):
                    wait_snap(sync, sn2)

        def routing(SLOT_, KG_, tag):
            sn = snap()
            with ExitStack() as c:
                IOE8 = sb(f"IOE8{tag}", [128, 1024], F32, c)
                IOE2K = sb(f"IOE2K{tag}", [128, 1024], F32, c)
                RSH = sb(f"RSH{tag}", [128, 1024], F32, c)
                RG8 = sb(f"RG8{tag}", [128, 1024], F32, c)
                REQ = sb(f"REQ{tag}", [128, 1024], F32, c)
                ROH = sb(f"ROH{tag}", [128, 1024], F32, c)
                RPOS = sb(f"RPOS{tag}", [128, 1024], F32, c)
                RKEEP = sb(f"RKEEP{tag}", [128, 1024], F32, c)
                EXCLS = sb(f"EXCL{tag}", [128, 1024], F32, c)
                CHSC = sb(f"CHSC{tag}", [128, 8], F32, c)
                PREFS = sb(f"PREFS{tag}", [128, 8], F32, c)
                PREFROW = sb(f"PREFR{tag}", [1, 1024], F32, c)
                CHSROW = sb(f"CHSROWr{tag}", [1, 1024], F32, c)
                RKEPT = sb(f"RKEPT{tag}", [128, NT], F32, c)
                RSLOTF = sb(f"RSLOTF{tag}", [128, NT], F32, c)
                RNK = sb(f"RNK{tag}", [128, NT], F32, c)
                PEX = c.enter_context(nc.psum_tensor(f"PEX{tag}", [128, 1024], F32))
                PCH = c.enter_context(nc.psum_tensor(f"PCH{tag}", [1, 1024], F32))
                PPF = c.enter_context(nc.psum_tensor(f"PPF{tag}", [128, 8], F32))
                RNM, RSE, RGT, RIDXF = TMPC[0], TMPC[1], TMPC[2], TMPC[3]

                @block.gpsimd
                def _(g):
                    wait_snap(g, sn)
                    g.iota(IOE8[:].rearrange("p (c e) -> p c e", e=8), [[0, NT], [1, 8]],
                           base=-8, channel_multiplier=0,
                           allow_small_or_imprecise_dtypes=True).then_inc(*sP.inc(1))
                    g.iota(IOE2K[:].rearrange("p (c e) -> p c e", e=8), [[0, NT], [CAP, 8]],
                           base=0, channel_multiplier=0,
                           allow_small_or_imprecise_dtypes=True).then_inc(*sP.inc(1))
                p_io = sP.n

                LOG3 = LOG[:].rearrange("p (c e) -> p c e", e=8)

                @block.vector
                def _(v):
                    def fence():
                        v.wait_ge(sV.h, sV.n)
                    wait_snap(v, sn)
                    v.wait_ge(sP.h, p_io)
                    v.tensor_reduce(RNM[:], LOG3, AX.X, ALU.max, negate=True).then_inc(*sV.inc(1))
                    NM3 = RNM[:].unsqueeze(2).broadcast_to([128, NT, 8])
                    fence()
                    v.scalar_tensor_tensor(RSH[:].rearrange("p (c e) -> p c e", e=8), LOG3, 1.0,
                                           NM3, ALU.mult, ALU.add).then_inc(*sV.inc(1))
                v_sh = sV.n

                @block.scalar
                def _(s):
                    wait_snap(s, sn)
                    s.wait_ge(sV.h, v_sh)
                    s.activation(RG8[:], RSH[:], AF.Exp).then_inc(*sA.inc(1))
                a_g8 = sA.n

                @block.vector
                def _(v):
                    def fence():
                        v.wait_ge(sV.h, sV.n)
                    v.wait_ge(sA.h, a_g8)
                    v.tensor_reduce(RSE[:], RG8[:].rearrange("p (c e) -> p c e", e=8), AX.X,
                                    ALU.add).then_inc(*sV.inc(1))
                    fence()
                    v.reciprocal(RGT[:], RSE[:]).then_inc(*sV.inc(1))
                    v.tensor_single_scalar(REQ[:], RSH[:], 0.0, ALU.is_equal).then_inc(*sV.inc(1))
                    fence()
                    v.tensor_mul(RG8[:], REQ[:], IOE8[:]).then_inc(*sV.inc(1))
                    fence()
                    v.tensor_reduce(RIDXF[:], RG8[:].rearrange("p (c e) -> p c e", e=8), AX.X,
                                    ALU.min).then_inc(*sV.inc(1))
                    IDXF3 = RIDXF[:].unsqueeze(2).broadcast_to([128, NT, 8])
                    fence()
                    v.scalar_tensor_tensor(ROH[:].rearrange("p (c e) -> p c e", e=8),
                                           IOE8[:].rearrange("p (c e) -> p c e", e=8), 1.0, IDXF3,
                                           ALU.mult, ALU.is_equal).then_inc(*sV.inc(1))
                v_oh = sV.n

                @block.tensor
                def _(t):
                    wait_snap(t, sn)
                    t.wait_ge(sV.h, v_oh)
                    for sub in range(2):
                        t.matmul(PEX[:, sub * 512:(sub + 1) * 512], SUT[:],
                                 ROH[:, sub * 512:(sub + 1) * 512], start=True, stop=True)
                        mm = t.matmul(PCH[:, sub * 512:(sub + 1) * 512], ONES[:, 0:1],
                                      ROH[:, sub * 512:(sub + 1) * 512], start=True, stop=True)
                    mm.then_inc(*sT.inc(1))
                t_ex = sT.n

                @block.vector
                def _(v):
                    v.wait_ge(sT.h, t_ex)
                    v.tensor_copy(EXCLS[:], PEX[:]).then_inc(*sV.inc(1))
                    v.tensor_copy(CHSROW[0:1, :], PCH[0:1, :]).then_inc(*sV.inc(1))
                v_ch = sV.n

                tk = {}

                @block.sync
                def _(sync):
                    wait_snap(sync, sn)
                    sync.wait_ge(sV.h, v_ch)
                    t1 = dma(sync, QP, sml_bounce[0:1024], CHSROW[0:1, :])
                    sync.wait_ge(*t1)
                    tk["chsc"] = dma(sync, QP, CHSC[:],
                                     sml_bounce[0:1024].rearrange("(c e) -> c e", e=8))

                @block.tensor
                def _(t):
                    t.wait_ge(*tk["chsc"])
                    t.matmul(PPF[:], SUT[:], CHSC[:], start=True, stop=True).then_inc(*sT.inc(1))
                t_pf = sT.n

                @block.vector
                def _(v):
                    v.wait_ge(sT.h, t_pf)
                    v.tensor_copy(PREFS[:], PPF[:]).then_inc(*sV.inc(1))
                v_pf = sV.n

                @block.sync
                def _(sync):
                    sync.wait_ge(sV.h, v_pf)
                    t1 = dma(sync, QP, sml_bounce[0:1024].rearrange("(c e) -> c e", e=8),
                             PREFS[:])
                    sync.wait_ge(*t1)
                    tk["pref"] = dma(sync, QP, PREFROW[0:1, :],
                                     sml_bounce[0:1024].unsqueeze(0))

                @block.tensor
                def _(t):
                    t.wait_ge(*tk["pref"])
                    for sub in range(2):
                        mm = t.matmul(PEX[:, sub * 512:(sub + 1) * 512], ONES[0:1, :],
                                      PREFROW[0:1, sub * 512:(sub + 1) * 512],
                                      start=True, stop=True)
                    mm.then_inc(*sT.inc(1))
                t_pb = sT.n

                @block.vector
                def _(v):
                    def fence():
                        v.wait_ge(sV.h, sV.n)
                    v.wait_ge(sT.h, t_pb)
                    v.tensor_add(RPOS[:], EXCLS[:], PEX[:]).then_inc(*sV.inc(1))
                    fence()
                    v.scalar_tensor_tensor(RKEEP[:], RPOS[:], float(CAP), ROH[:],
                                           ALU.is_lt, ALU.mult).then_inc(*sV.inc(1))
                    fence()
                    v.tensor_reduce(RKEPT[:], RKEEP[:].rearrange("p (c e) -> p c e", e=8), AX.X,
                                    ALU.add).then_inc(*sV.inc(1))
                    v.tensor_add(RSH[:], RPOS[:], IOE2K[:]).then_inc(*sV.inc(1))
                    fence()
                    v.tensor_mul(RSH[:], RSH[:], RKEEP[:]).then_inc(*sV.inc(1))
                    fence()
                    v.tensor_reduce(RSLOTF[:], RSH[:].rearrange("p (c e) -> p c e", e=8), AX.X,
                                    ALU.add).then_inc(*sV.inc(1))
                    v.tensor_scalar(RNK[:], RKEPT[:], -1.0, 1.0, ALU.mult, ALU.add).then_inc(*sV.inc(1))
                    fence()
                    v.tensor_mul(RNK[:], TOKB[:], RNK[:]).then_inc(*sV.inc(1))
                    fence()
                    v.tensor_add(RSLOTF[:], RSLOTF[:], RNK[:]).then_inc(*sV.inc(1))
                    fence()
                    v.tensor_copy(SLOT_[:], RSLOTF[:]).then_inc(*sV.inc(1))
                    v.scalar_tensor_tensor(KG_[:], RGT[:], 0.5, RKEPT[:],
                                           ALU.mult, ALU.mult).then_inc(*sV.inc(1))
                sn2 = snap()

                @block.sync
                def _(sync):
                    wait_snap(sync, sn2)

        def scatter_pass(src_dram, MU_, RSTD_, SLOT_, xg_dram, tag):
            sn = snap()
            v0 = sV.n
            NBLK = NT // 8
            ld_toks = [None] * NBLK
            sc_toks = [None] * NT
            with ExitStack() as c:
                SRC = [sb(f"SCSRC{i}{tag}", [128, 8 * 1024], F32, c) for i in range(2)]
                NB = [BP[0], BP[1], BP[2], BP[3]] + [
                    sb(f"SCNB{i}{tag}", [128, 1024], BF16, c) for i in range(4)]

                @block.sync
                def _(sync):
                    wait_snap(sync, sn)
                    for k in range(NBLK):
                        if k >= 2:
                            sync.wait_ge(sV.h, v0 + 8 * (k - 1))
                        ld_toks[k] = dma(sync, QP,
                                         SRC[k % 2][:].rearrange("p (t d) -> p t d", d=1024),
                                         src_dram[k * 1024:(k + 1) * 1024, :]
                                         .rearrange("(t p) d -> p t d", p=128))

                @block.gpsimd
                def _(g):
                    wait_snap(g, sn)
                    for i in range(NT):
                        g.wait_ge(sV.h, v0 + i + 1)
                        inc, tok = GP.issue(g)
                        g.indirect_dma_start(out=xg_dram[:],
                                             out_offset=IndirectOffsetOnAxis(SLOT_[:, i:i + 1], 0),
                                             in_=NB[i % 8][:], in_offset=None,
                                             bounds_check=r_bnd[0], oob_is_err=False).then_inc(*inc)
                        sc_toks[i] = tok

                @block.vector
                def _(v):
                    wait_snap(v, sn)
                    for i in range(NT):
                        if i % 8 == 0:
                            v.wait_ge(*ld_toks[i // 8])
                        if i >= 8:
                            v.wait_ge(*sc_toks[i - 8])
                        sl = SRC[(i // 8) % 2][:, (i % 8) * 1024:(i % 8 + 1) * 1024]
                        v.tensor_scalar(NB[i % 8][:], sl, MU_[:, i:i + 1],
                                        RSTD_[:, i:i + 1], ALU.subtract, ALU.mult).then_inc(*sV.inc(1))
                sn2x = snap()

                @block.sync
                def _(sync):
                    wait_snap(sync, sn2x)

        def expert_pass(xg_dram, y_dram, first_call, tag):
            sn = snap()
            with ExitStack() as c:
                XGT = [sb(f"XGT{i}{tag}", [128, 8 * 512], BF16, c) for i in range(2)]
                W2RES = sb(f"W2RES{tag}", [128, 32 * 1024], BF16, c)
                H1G = sb(f"H1G{tag}", [128, 32 * 512], BF16, c)
                W1S = [sb(f"W1S{i}{tag}", [128, 8 * 512], BF16, c) for i in range(2)]
                YROW = [sb(f"YROW{i}{tag}", [128, 1024], BF16, c) for i in range(2)]
                PH1 = []
                PH1.append(c.enter_context(nc.psum_tensor(f"PH10{tag}", [128, 512], F32)))
                PPAD = c.enter_context(nc.psum_tensor(f"PPAD{tag}", [128, 512], F32))
                PH1.append(c.enter_context(nc.psum_tensor(f"PH11{tag}", [128, 512], F32)))
                PY = [c.enter_context(nc.psum_tensor(f"PYe{i}{tag}", [128, 1024], F32))
                      for i in range(2)]

                t_prev_end = [sT.n]
                t_w1_prevq = [sT.n]
                ystore_toks = [[]]
                for e in range(E):
                    w2_toks = []
                    for q in range(4):
                        xgt_toks = []

                        @block.sync
                        def _(sync, e=e, q=q, xgt_toks=xgt_toks):
                            if e == 0 and q == 0:
                                wait_snap(sync, sn)
                            sync.wait_ge(sT.h, t_w1_prevq[0])
                            r0 = e * CAP + q * 512
                            for dc in range(8):
                                xgt_toks.append(dmat_(sync, QP,
                                                      XGT[q % 2][:, dc * 512:(dc + 1) * 512],
                                                      xg_dram[r0:r0 + 512,
                                                              dc * 128:(dc + 1) * 128]))

                        t0 = sT.n
                        v0 = sV.n
                        a0 = sA.n
                        w1c_toks = [None] * 8
                        tpe = t_prev_end[0]

                        @block.gpsimd
                        def _(g, e=e, q=q, t0=t0, w1c_toks=w1c_toks,
                              w2_toks=w2_toks, tpe=tpe):
                            if e == 0 and q == 0:
                                wait_snap(g, sn)
                            if q == 0:
                                g.wait_ge(sT.h, tpe)
                                if first_call:
                                    for tkx in cast_toks[f"w2b{e}"]:
                                        g.wait_ge(*tkx)
                                    for tkx in cast_toks[f"w1b{e}"]:
                                        g.wait_ge(*tkx)
                                for j in range(8):
                                    w2_toks.append(dma(
                                        g, GP,
                                        W2RES[:, j * 4096:(j + 1) * 4096]
                                        .rearrange("p (hc d) -> p hc d", d=1024),
                                        w2b[e, j * 512:(j + 1) * 512, :]
                                        .rearrange("(hc p) d -> p hc d", p=128)))
                            for cb in range(8):
                                lim = t0 + 4 * cb - 8 if cb < 2 else t0 + 4 * (cb - 1)
                                if lim > 0:
                                    g.wait_ge(sT.h, lim)
                                w1c_toks[cb] = dma(
                                    g, GP,
                                    W1S[cb % 2][:].rearrange("p (dc h) -> p dc h", h=512),
                                    w1b[e].rearrange("(dc p) h -> p dc h", p=128)
                                    [:, :, cb * 512:(cb + 1) * 512])
                            if first_call and e + 1 < E:
                                cast_toks.setdefault(f"w1b{e + 1}", []).append(
                                    dma(g, GP, w1b[e + 1, q * 256:(q + 1) * 256, :],
                                        w1_d[e + 1, q * 256:(q + 1) * 256, :]))
                                cast_toks.setdefault(f"w2b{e + 1}", []).append(
                                    dma(g, GP, w2b[e + 1, q * 1024:(q + 1) * 1024, :],
                                        w2_d[e + 1, q * 1024:(q + 1) * 1024, :]))

                        @block.tensor
                        def _(t, e=e, q=q, xgt_toks=xgt_toks, w1c_toks=w1c_toks, a0=a0):
                            if e == 0 and q == 0:
                                wait_snap(t, sn)
                            for tkx in xgt_toks:
                                t.wait_ge(*tkx)
                            t.wait_ge(sA.h, a0)
                            for ht in range(32):
                                if ht % 4 == 0:
                                    t.wait_ge(*w1c_toks[ht // 4])
                                if ht >= 2:
                                    t.wait_ge(sA.h, a0 + ht - 1)
                                ph = PH1[ht % 2]
                                wv = W1S[(ht // 4) % 2]
                                hoff = (ht % 4) * 128
                                for dc in range(8):
                                    mm = t.matmul(ph[:],
                                                  wv[:, dc * 512 + hoff:dc * 512 + hoff + 128],
                                                  XGT[q % 2][:, dc * 512:(dc + 1) * 512],
                                                  start=(dc == 0), stop=(dc == 7))
                                mm.then_inc(*sT.inc(1))
                        t_w1_prevq[0] = sT.n

                        @block.scalar
                        def _(s, e=e, q=q, t0=t0):
                            if e == 0 and q == 0:
                                wait_snap(s, sn)
                            for ht in range(32):
                                s.wait_ge(sT.h, t0 + ht + 1)
                                s.activation(H1G[:, ht * 512:(ht + 1) * 512],
                                             PH1[ht % 2][:],
                                             AF.Gelu_apprx_tanh).then_inc(*sA.inc(1))
                        gelu_a = sA.n

                        t1 = sT.n
                        v1 = sV.n
                        yst_prev = ystore_toks[0]
                        ystore_toks[0] = []

                        @block.tensor
                        def _(t, gelu_a=gelu_a, w2_toks=w2_toks, v1=v1):
                            t.wait_ge(sA.h, gelu_a)
                            for tkx in w2_toks:
                                t.wait_ge(*tkx)
                            for st in range(4):
                                if st >= 2:
                                    t.wait_ge(sV.h, v1 + (st - 1))
                                py = PY[st % 2]
                                for hc in range(32):
                                    for sub in range(2):
                                        mm = t.matmul(py[:, sub * 512:(sub + 1) * 512],
                                                      H1G[:, hc * 512 + st * 128:
                                                          hc * 512 + st * 128 + 128],
                                                      W2RES[:, hc * 1024 + sub * 512:
                                                            hc * 1024 + (sub + 1) * 512],
                                                      start=(hc == 0), stop=(hc == 31))
                                mm.then_inc(*sT.inc(1))
                        t_prev_end[0] = sT.n

                        @block.sync
                        def _(sync, e=e, q=q, v1=v1):
                            for st in range(4):
                                sync.wait_ge(sV.h, v1 + st + 1)
                                r0 = e * CAP + q * 512 + st * 128
                                ystore_toks[0].append(dma(sync, QP, y_dram[r0:r0 + 128, :],
                                                          YROW[st % 2][:]))

                        @block.vector
                        def _(v, t1=t1, yst_prev=yst_prev, cur=ystore_toks[0]):
                            for st in range(4):
                                v.wait_ge(sT.h, t1 + st + 1)
                                if st < 2:
                                    for tkx in yst_prev[st::2]:
                                        v.wait_ge(*tkx)
                                else:
                                    v.wait_ge(*cur[st - 2])
                                v.tensor_copy(YROW[st % 2][:], PY[st % 2][:]).then_inc(*sV.inc(1))
                sn2 = snap()

                @block.sync
                def _(sync):
                    wait_snap(sync, sn2)

        def combine_pass(src_dram, y_dram, SLOT_, KG_, dst_dram, want_stats, tag):
            sn = snap()
            v0, a0 = sV.n, sA.n
            nvp = 2 if want_stats else 1
            NBLK = NT // 8
            g_toks = [None] * NT
            l_toks = [None] * NBLK
            s_toks = [None] * NBLK
            with ExitStack() as c:
                SRC = [sb(f"CBSRC{i}{tag}", [128, 8 * 1024], F32, c) for i in range(2)]
                OUT = [sb(f"CBOUT{i}{tag}", [128, 8 * 1024], F32, c) for i in range(2)]
                GB = [BP[0], BP[1], BP[2], BP[3]] + [
                    sb(f"CBGB{i}{tag}", [128, 1024], BF16, c) for i in range(4)]

                @block.gpsimd
                def _(g):
                    wait_snap(g, sn)
                    for i in range(NT):
                        if i >= 8:
                            g.wait_ge(sV.h, v0 + (i - 8) * nvp + 1)
                        inc, tok = GP.issue(g)
                        g.indirect_dma_start(out=GB[i % 8][:], out_offset=None,
                                             in_=y_dram[:],
                                             in_offset=IndirectOffsetOnAxis(SLOT_[:, i:i + 1], 0),
                                             bounds_check=r_bnd[0], oob_is_err=False).then_inc(*inc)
                        g_toks[i] = tok

                @block.sync
                def _(sync):
                    wait_snap(sync, sn)
                    for k in range(NBLK):
                        if k >= 2:
                            sync.wait_ge(sV.h, v0 + 8 * (k - 1) * nvp + 1)
                        l_toks[k] = dma(sync, QP,
                                        SRC[k % 2][:].rearrange("p (t d) -> p t d", d=1024),
                                        src_dram[k * 1024:(k + 1) * 1024, :]
                                        .rearrange("(t p) d -> p t d", p=128))

                @block.scalar
                def _(s):
                    wait_snap(s, sn)
                    for i in range(NT + 8):
                        if i < NT and want_stats:
                            s.wait_ge(sV.h, v0 + i * nvp + 1)
                            if i >= 1:
                                s.wait_ge(sA.h, sA.n)
                            b, t = i // 8, i % 8
                            s.activation(SCRA[:], OUT[b % 2][:, t * 1024:(t + 1) * 1024],
                                         AF.Identity,
                                         accum_out=SUMS[:, i:i + 1]).then_inc(*sA.inc(1))
                        if i >= 8 and i % 8 == 0:
                            k = i // 8 - 1
                            sc = sA.n
                            s.wait_ge(sV.h, v0 + 8 * (k + 1) * nvp - nvp + 1)
                            s_toks[k] = dma(s, SP, dst_dram[k * 1024:(k + 1) * 1024, :]
                                            .rearrange("(t p) d -> p t d", p=128),
                                            OUT[k % 2][:].rearrange("p (t d) -> p t d", d=1024))

                @block.vector
                def _(v):
                    wait_snap(v, sn)
                    for i in range(NT):
                        if i % 8 == 0:
                            v.wait_ge(*l_toks[i // 8])
                        v.wait_ge(*g_toks[i])
                        if i >= 16:
                            if i % 8 == 0:
                                v.wait_ge(*s_toks[i // 8 - 2])
                            if want_stats:
                                v.wait_ge(sA.h, a0 + (i - 1))
                        b, t = i // 8, i % 8
                        ob = OUT[b % 2][:, t * 1024:(t + 1) * 1024]
                        v.scalar_tensor_tensor(ob, GB[i % 8][:], KG_[:, i:i + 1],
                                               SRC[b % 2][:, t * 1024:(t + 1) * 1024],
                                               ALU.mult, ALU.add).then_inc(*sV.inc(1))
                        if want_stats:
                            v.wait_ge(sV.h, sV.n)
                            v.scalar_tensor_tensor(SCRV[:], ob, 1.0, ob,
                                                   ALU.mult, ALU.mult,
                                                   accum_out=SSQ[:, i:i + 1]).then_inc(*sV.inc(1))
                sn2x = snap()

                @block.sync
                def _(sync):
                    wait_snap(sync, sn2x)

        stats_pass(x_d)
        stats_finalize(MU, RSTD, NEGR_=NEGR)
        logits_pass(xbd, MU, NEGR, 'a')
        if debug:
            o_log = dbg_out("o_log", [128, 1024])
            o_mu = dbg_out("o_mu", [128, NT])
            o_rstd = dbg_out("o_rstd", [128, NT])
            sn = snap()

            @block.sync
            def _(sync):
                wait_snap(sync, sn)
                for pair in [(o_log, LOG), (o_mu, MU), (o_rstd, RSTD)]:
                    tok = dma(sync, QP, pair[0][:], pair[1][:])
                    sync.wait_ge(*tok)
        if n_phases < 2:
            finalize()
            return nc, dbg

        routing(SLOT1, KG1, "r1")
        if debug:
            o_slot1 = dbg_out("o_slot1", [128, NT], U32)
            o_kg1 = dbg_out("o_kg1", [128, NT])
            sn = snap()

            @block.sync
            def _(sync):
                wait_snap(sync, sn)
                for pair in [(o_slot1, SLOT1), (o_kg1, KG1)]:
                    tok = dma(sync, QP, pair[0][:], pair[1][:])
                    sync.wait_ge(*tok)
        if n_phases < 3:
            finalize()
            return nc, dbg

        scatter_pass(x_d, MU, RSTD, SLOT1, xg1, 'a')
        if n_phases < 4:
            finalize()
            return nc, dbg

        expert_pass(xg1, y1d, first_call=True, tag='a')
        if debug:
            o_y1 = dbg_out("o_y1", [256, D], BF16)
            sn = snap()

            @block.sync
            def _(sync):
                wait_snap(sync, sn)
                tok = dma(sync, QP, o_y1[:], y1d[0:256, :])
                sync.wait_ge(*tok)
        if n_phases < 5:
            finalize()
            return nc, dbg

        combine_pass(x_d, y1d, SLOT1, KG1, x1_dram, want_stats=True, tag='a')
        stats_finalize(MU1, RSTD1, S1_=S1, S2_=S2, R2_=RSTD2, F1_=MU2)
        if debug:
            o_x1 = dbg_out("o_x1", [256, D])
            sn = snap()

            @block.sync
            def _(sync):
                wait_snap(sync, sn)
                tok = dma(sync, QP, o_x1[:], x1_dram[0:256, :])
                sync.wait_ge(*tok)
        if n_phases < 6:
            finalize()
            return nc, dbg

        sn6c = snap()
        v6c = sV.n
        NBLK6 = NT // 8
        toks6c = [None] * NBLK6
        st6c = [None] * NBLK6
        with ExitStack() as c6x:
            SRC6 = [sb(f"P6SRC{i}", [128, 8 * 1024], F32, c6x) for i in range(2)]
            NB6 = [sb(f"P6DST{i}", [128, 8 * 1024], BF16, c6x) for i in range(2)]

            @block.sync
            def _(sync):
                wait_snap(sync, sn6c)
                for k in range(NBLK6):
                    if k >= 2:
                        sync.wait_ge(sV.h, v6c + 8 * (k - 1))
                    toks6c[k] = dma(sync, QP,
                                    SRC6[k % 2][:].rearrange("p (t d) -> p t d", d=1024),
                                    x1_dram[k * 1024:(k + 1) * 1024, :]
                                    .rearrange("(t p) d -> p t d", p=128))

            @block.scalar
            def _(s):
                wait_snap(s, sn6c)
                for k in range(NBLK6):
                    s.wait_ge(sV.h, v6c + 8 * (k + 1))
                    st6c[k] = dma(s, SP, xn2b[k * 1024:(k + 1) * 1024, :]
                                  .rearrange("(t p) d -> p t d", p=128),
                                  NB6[k % 2][:].rearrange("p (t d) -> p t d", d=1024))

            @block.vector
            def _(v):
                wait_snap(v, sn6c)
                for i in range(NT):
                    if i % 8 == 0:
                        v.wait_ge(*toks6c[i // 8])
                        if i >= 16:
                            v.wait_ge(*st6c[i // 8 - 2])
                    b, t = i // 8, i % 8
                    v.tensor_scalar(NB6[b % 2][:, t * 1024:(t + 1) * 1024],
                                    SRC6[b % 2][:, t * 1024:(t + 1) * 1024],
                                    MU1[:, i:i + 1],
                                    MU2[:, i:i + 1], ALU.subtract, ALU.mult).then_inc(*sV.inc(1))
            sn6e = snap()

            @block.sync
            def _(sync):
                wait_snap(sync, sn6e)

        if n_phases < 7:
            finalize()
            return nc, dbg

        sn7 = snap()
        with ExitStack() as c7:
            WINT = sb("WINT7", [128, 8 * 2048], BF16, c7)
            WXPT = sb("WXPT", [128, 8 * 64], BF16, c7)
            WOUTT = sb("WOUTT", [128, 8 * 1024], BF16, c7)
            BOUTB = sb("BOUTB", [128, 1024], F32, c7)
            XTS = [sb(f"AXT{i}", [128, 2048], BF16, c7) for i in range(8)]
            YSC = XTS
            XPS = [sb(f"XPS{i}", [128, 2048], BF16, c7) for i in range(2)]
            ZS8 = [sb(f"ZS8{i}", [128, 2048], BF16, c7) for i in range(8)]
            USTRIP = sb("USTRIP", [64, 2048], F32, c7)
            HBS = sb("HBS", [64, 2048], BF16, c7)
            CARRY = sb("CARRY", [64, 1], F32, c7)
            X2BB = [sb(f"X2BB{i}", [128, 1024], BF16, c7) for i in range(2)]
            PY2 = [c7.enter_context(nc.psum_tensor(f"PY2{i}", [128, 1024], F32))
                   for i in range(2)]
            PBIG = c7.enter_context(nc.psum_tensor("PBIG", [128, 2048], F32))
            YBB = [SCRA, SCRV]

            wtoks = []

            @block.sync
            def _(sync):
                wait_snap(sync, sn7)
                wtoks.append(dma(sync, QP, WINT[:], wintp[:]))
                sync.wait_ge(*cast_toks["wxpb"])
                for ft in range(8):
                    wtoks.append(dmat_(sync, QP, WXPT[:, ft * 64:(ft + 1) * 64],
                                       wxpb[:, ft * 128:(ft + 1) * 128]))
                sync.wait_ge(*cast_toks["woutb"])
                for mc in range(8):
                    wtoks.append(dmat_(sync, QP, WOUTT[:, mc * 1024:(mc + 1) * 1024],
                                       woutb[:, mc * 128:(mc + 1) * 128]))
                wtoks.append(dma(sync, QP, BOUTB[:], boutp[:]))

            @block.vector
            def _(v):
                wait_snap(v, sn7)
                v.memset(CARRY[:], 0.0).then_inc(*sV.inc(1))
            v_init = sV.n

            t_prev = [sT.n]
            a_prev_end = [sA.n]
            dbg_gate = [None]
            for strip in range(8):
                xts_toks = []

                @block.sync
                def _(sync, strip=strip, xts_toks=xts_toks):
                    if strip == 0:
                        for tkx in wtoks:
                            sync.wait_ge(*tkx)
                    else:
                        sync.wait_ge(sT.h, t_prev[0])
                    for dc in range(8):
                        xts_toks.append(dmat_(sync, QP, XTS[dc][:],
                                              xn2b[strip * 2048:(strip + 1) * 2048,
                                                   dc * 128:(dc + 1) * 128]))

                t0 = sT.n
                a0 = sA.n
                v0 = sV.n
                vS = v0 + 3

                def v_fst(i, v0=v0):
                    return v0 + 12 + 3 * i

                def v_snd(i, v0=v0):
                    return v0 + 13 + 3 * i

                def v_fin(i, v0=v0):
                    return v0 + 14 + 3 * i

                x1_toks = [None] * 16
                x2_toks = [None] * 16
                x2b_toks = [None] * 16

                @block.sync
                def _(sync, strip=strip, v0=v0, a0=a0, v_fst=v_fst, v_fin=v_fin,
                      x1_toks=x1_toks, x2_toks=x2_toks, x2b_toks=x2b_toks):
                    for i in range(17):
                        if i < 16:
                            if i >= 2:
                                sync.wait_ge(sV.h, v_fst(i - 2))
                            gi = strip * 16 + i
                            x1_toks[i] = dma(sync, QP, XBUF[i % 2][:],
                                             x1_dram[gi * 128:(gi + 1) * 128, :])
                        if i >= 1:
                            j = i - 1
                            gj = strip * 16 + j
                            sync.wait_ge(sV.h, v_fin(j))
                            x2_toks[j] = dma(sync, QP, x2_dram[gj * 128:(gj + 1) * 128, :],
                                             X1BUF[j % 2][:])
                            sync.wait_ge(sA.h, a0 + 68 + 2 * j + 1)
                            x2b_toks[j] = dma(sync, QP, x2b[gj * 128:(gj + 1) * 128, :],
                                              X2BB[j % 2][:])

                @block.tensor
                def _(t, strip=strip, xts_toks=xts_toks, t0=t0, a0=a0, v0=v0,
                      ape=a_prev_end[0], vS=vS, v_snd=v_snd):
                    if strip == 0:
                        wait_snap(t, sn7)
                    else:
                        t.wait_ge(sV.h, v0)
                    for tkx in xts_toks:
                        t.wait_ge(*tkx)
                    t.wait_ge(sA.h, ape)
                    r = 0
                    for ft in range(16):
                        for sub in range(4):
                            if r >= 2:
                                t.wait_ge(sA.h, a0 + (r - 1) if r < 34 else a0 + r + 3)
                            ph = PY2[r % 2][:, 0:512]
                            for dc in range(8):
                                mm = t.matmul(ph,
                                              WINT[:, dc * 2048 + ft * 128:
                                                   dc * 2048 + (ft + 1) * 128],
                                              XTS[dc][:, sub * 512:(sub + 1) * 512],
                                              start=(dc == 0), stop=(dc == 7))
                            mm.then_inc(*sT.inc(1))
                            r += 1
                        if ft < 8:
                            t.wait_ge(sA.h, a0 + 4 * (ft + 1))
                            for sub in range(4):
                                mm = t.matmul(PBIG[0:64, sub * 512:(sub + 1) * 512],
                                              WXPT[:, ft * 64:(ft + 1) * 64],
                                              XPS[ft % 2][:, sub * 512:(sub + 1) * 512],
                                              start=(ft == 0), stop=(ft == 7))
                            mm.then_inc(*sT.inc(1))
                    t.wait_ge(sV.h, vS)
                    for mc in range(8):
                        if mc >= 1:
                            t.wait_ge(sV.h, vS + mc)
                        for sub in range(4):
                            mm = t.matmul(PBIG[:, sub * 512:(sub + 1) * 512],
                                          DMATB[:, mc * 128:(mc + 1) * 128],
                                          HBS[:, sub * 512:(sub + 1) * 512],
                                          start=True, stop=True)
                        mm.then_inc(*sT.inc(1))
                    for i in range(16):
                        if i == 0:
                            t.wait_ge(sV.h, vS + 8)
                        if i >= 2:
                            t.wait_ge(sV.h, v_snd(i - 2))
                        for mc in range(8):
                            for sub in range(2):
                                mm = t.matmul(PY2[i % 2][:, sub * 512:(sub + 1) * 512],
                                              YSC[mc][:, i * 128:(i + 1) * 128],
                                              WOUTT[:, mc * 1024 + sub * 512:
                                                    mc * 1024 + (sub + 1) * 512],
                                              start=(mc == 0), stop=(mc == 7))
                        mm.then_inc(*sT.inc(1))
                t_prev[0] = sT.n
                t_hd0 = t0 + 72
                t_y0 = t0 + 80

                @block.scalar
                def _(s, strip=strip, t0=t0, a0=a0, v_fin=v_fin, x2b_toks=x2b_toks):
                    if strip == 0:
                        wait_snap(s, sn7)
                    if dbg_gate[0] is not None:
                        for tkx in dbg_gate[0]:
                            s.wait_ge(*tkx)
                    r = 0
                    for ft in range(8):
                        for sub in range(4):
                            s.wait_ge(sT.h, t0 + 5 * ft + sub + 1)
                            s.activation(XPS[ft % 2][:, sub * 512:(sub + 1) * 512],
                                         PY2[r % 2][0:128, 0:512], AF.Silu,
                                         bias=BINPP[:, ft:ft + 1]).then_inc(*sA.inc(1))
                            r += 1
                    s.wait_ge(sT.h, t0 + 40)
                    for sub in range(4):
                        s.activation(USTRIP[:, sub * 512:(sub + 1) * 512],
                                     PBIG[0:64, sub * 512:(sub + 1) * 512], AF.Identity,
                                     bias=BXP[:]).then_inc(*sA.inc(1))
                    for ft in range(8, 16):
                        for sub in range(4):
                            s.wait_ge(sT.h, t0 + 40 + 4 * (ft - 8) + sub + 1)
                            s.activation(ZS8[ft - 8][:, sub * 512:(sub + 1) * 512],
                                         PY2[r % 2][0:128, 0:512], AF.Sigmoid,
                                         bias=BINPP[:, ft:ft + 1]).then_inc(*sA.inc(1))
                            r += 1
                    for i in range(16):
                        gi = strip * 16 + i
                        s.wait_ge(sV.h, v_fin(i))
                        if i >= 2:
                            s.wait_ge(*x2b_toks[i - 2])
                        s.activation(X2BB[i % 2][:], X1BUF[i % 2][:], AF.Identity,
                                     accum_out=SUMS[:, gi:gi + 1]).then_inc(*sA.inc(1))
                        s.activation(TBUF[i % 2][:], X1BUF[i % 2][:], AF.Square,
                                     accum_out=SSQ[:, gi:gi + 1]).then_inc(*sA.inc(1))
                a_prev_end[0] = a0 + 68

                @block.vector
                def _(v, strip=strip, t0=t0, a0=a0, v0=v0,
                      t_hd0=t_hd0, t_y0=t_y0, x1_toks=x1_toks, x2_toks=x2_toks):
                    if strip == 0:
                        wait_snap(v, sn7)
                        v.wait_ge(sV.h, v_init)
                    if dbg_gate[0] is not None:
                        for tkx in dbg_gate[0]:
                            v.wait_ge(*tkx)
                        dbg_gate[0] = None
                    v.wait_ge(sA.h, a0 + 36)
                    ab = AA[:].broadcast_to([64, 2048])
                    v.tensor_tensor_scan(USTRIP[:], ab, USTRIP[:], CARRY[:],
                                         ALU.mult, ALU.add).then_inc(*sV.inc(1))
                    v.wait_ge(sV.h, sV.n)
                    v.tensor_copy(CARRY[:], USTRIP[:, 2047:2048]).then_inc(*sV.inc(1))
                    v.tensor_copy(HBS[:], USTRIP[:]).then_inc(*sV.inc(1))
                    for mc in range(8):
                        v.wait_ge(sT.h, t_hd0 + mc + 1)
                        v.wait_ge(sA.h, a0 + 36 + 4 * (mc + 1))
                        v.tensor_mul(YSC[mc][:], PBIG[:], ZS8[mc][:]).then_inc(*sV.inc(1))
                    for i in range(16):
                        gi = strip * 16 + i
                        v.wait_ge(sT.h, t_y0 + i + 1)
                        v.wait_ge(*x1_toks[i])
                        if i >= 2:
                            v.wait_ge(*x2_toks[i - 2])
                            v.wait_ge(sA.h, a0 + 68 + 2 * (i - 2) + 2)
                        v.tensor_scalar(TBUF[i % 2][:], XBUF[i % 2][:], S1[:, gi:gi + 1],
                                        S2[:, gi:gi + 1], ALU.mult,
                                        ALU.subtract).then_inc(*sV.inc(1))
                        v.tensor_add(YBB[i % 2][:], PY2[i % 2][:], BOUTB[:]).then_inc(*sV.inc(1))
                        v.wait_ge(sV.h, sV.n)
                        v.tensor_add(X1BUF[i % 2][:], TBUF[i % 2][:],
                                     YBB[i % 2][:]).then_inc(*sV.inc(1))

                if debug and strip == 0:
                    o_h = dbg_out("o_h", [64, 2048])
                    o_u = dbg_out("o_u", [64, 2048], BF16)
                    snd = snap()

                    @block.gpsimd
                    def _(g, snd=snd):
                        wait_snap(g, snd)
                        tk1 = dma(g, GP, o_h[:], USTRIP[:])
                        tk2 = dma(g, GP, o_u[:], HBS[:])
                        dbg_gate[0] = [tk1, tk2]
            sn7e = snap()

            @block.sync
            def _(sync):
                wait_snap(sync, sn7e)

        if n_phases < 9:
            finalize()
            return nc, dbg

        stats_finalize(MU3, RSTD3, NEGR_=NEGR3)
        if debug:
            o_x2 = dbg_out("o_x2", [256, D])
            sn = snap()

            @block.sync
            def _(sync):
                wait_snap(sync, sn)
                tok = dma(sync, QP, o_x2[:], x2_dram[0:256, :])
                sync.wait_ge(*tok)

        if n_phases < 10:
            finalize()
            return nc, dbg

        logits_pass(x2b, MU3, NEGR3, 'b')
        routing(SLOT2, KG2, "r2")
        scatter_pass(x2_dram, MU3, RSTD3, SLOT2, xg2, 'b')
        expert_pass(xg2, y2d, first_call=False, tag='b')
        if n_phases < 14:
            finalize()
            return nc, dbg

        combine_pass(x2_dram, y2d, SLOT2, KG2, x3_dram, want_stats=False, tag='b')
        snf = snap()

        @block.gpsimd
        def _(g):
            wait_snap(g, snf)
            r = g.alloc_register("r_rank")
            g.reg_load(r, RANKS[0:1, 0:1])
            inc, tok = GP.issue(g)
            for k in range(NCORES):
                with g.If_eq(r, k):
                    g.dma_start(out=out_d[:],
                                in_=x3_dram[k * CAP:(k + 1) * CAP, :]).then_inc(*inc)
            g.wait_ge(*tok)

        sn_end = snap()

        @block.sync
        def _(sync):
            wait_snap(sync, sn_end)

    return nc, dbg


def make_inputs(inputs, core):
    x = np.ascontiguousarray(np.asarray(inputs["L0_attn_x"], np.float32).reshape(N_TOK, D))
    gam = np.ascontiguousarray(np.asarray(inputs["L0_attn_gamma"], np.float32).reshape(8, 128).T)
    bet = np.ascontiguousarray(np.asarray(inputs["L0_attn_beta"], np.float32).reshape(8, 128).T)
    return dict(
        x=x,
        alog=np.asarray(inputs["L0_attn_A_log"], np.float32).reshape(64, 1),
        dmat=np.ascontiguousarray(np.asarray(inputs["L0_attn_Dmat"], np.float32)),
        win=np.ascontiguousarray(np.asarray(inputs["L0_attn_W_in"], np.float32)),
        bin=np.asarray(inputs["L0_attn_b_in"], np.float32).reshape(1, 2 * D),
        wxp=np.ascontiguousarray(np.asarray(inputs["L0_attn_W_xp"], np.float32)),
        bxp=np.asarray(inputs["L0_attn_b_xp"], np.float32).reshape(64, 1),
        wout=np.ascontiguousarray(np.asarray(inputs["L0_attn_W_out"], np.float32)),
        bout=np.asarray(inputs["L0_attn_b_out"], np.float32).reshape(1, D),
        gam=gam, bet=bet,
        wg=np.ascontiguousarray(np.asarray(inputs["L0_ffn_Wg"], np.float32)),
        w1=np.ascontiguousarray(np.asarray(inputs["L0_ffn_W1"], np.float32)),
        w2=np.ascontiguousarray(np.asarray(inputs["L0_ffn_W2"], np.float32)),
        ones=np.ones((128, 128), np.float32),
        bnd=np.array([[OOB]], dtype=np.uint32),
        zfill=np.zeros((2048, 1024), dtype=ml_dtypes.bfloat16),
        sut=np.ascontiguousarray(np.triu(np.ones((128, 128), np.float32), 1)),
        ident=np.eye(128, dtype=np.float32),
        rankin=np.array([[core]], dtype=np.uint32),
    )


def kernel(**inputs):
    nc, _ = build()
    in_maps = [make_inputs(inputs, c) for c in range(NCORES)]
    res = run_bass_kernel_spmd(nc, in_maps, core_ids=list(range(NCORES)))
    out = np.concatenate([np.asarray(res.results[c]["out"]) for c in range(NCORES)], axis=0)
    return out.reshape(1, N_TOK, D).astype(np.float32)
